# version 08
# speedup vs baseline: 1.3683x; 1.0558x over previous
import math
from contextlib import ExitStack

import numpy as np
import ml_dtypes

import concourse.bass as bass
import concourse.mybir as mybir
from concourse.bass_utils import run_bass_kernel_spmd

F32 = mybir.dt.float32
BF16 = mybir.dt.bfloat16
I32 = mybir.dt.int32
AF = mybir.ActivationFunctionType
ALU = mybir.AluOpType
AX = mybir.AxisListType

NCORES = 8
D = 2048
B = 2
S = 8192
PPC = S // NCORES
RPC = B * PPC
NT = RPC // 128
NCH = D // 128
HEADS = 16
HDIM = 128
MLP0 = 16384
H1 = 5120
NFC = H1 // 128
INVSQRTD = 1.0 / math.sqrt(HDIM)
EXPSHIFT = -12.0

ENGINES = ("sync", "gpsimd", "vector", "scalar", "tensor")


class Prog:

    def __init__(self, nc):
        self.nc = nc
        self.streams = {e: [] for e in ENGINES}
        self.semh = {}
        self.semc = {}
        self.pool = []
        self.phys_count = {}

    def sem(self, name):
        if name not in self.semh:
            if self.pool:
                h = self.pool.pop()
                self.semh[name] = h
                self.semc[name] = self.phys_count.get(id(h), 0)
            else:
                self.semh[name] = self.nc.alloc_semaphore(name)
                self.semc[name] = 0
        return self.semh[name]

    def retire_all_sems(self):
        for name, h in self.semh.items():
            self.phys_count[id(h)] = self.semc[name]
            self.pool.append(h)
        self.semh = {}
        self.semc = {}

    def reserve(self, name, n=1):
        self.sem(name)
        self.semc[name] += n
        return self.semc[name]

    def val(self, name):
        self.sem(name)
        return self.semc[name]

    def emit(self, stream, build, inc=None, waits=(), incn=1):
        if inc is not None:
            tgt = self.reserve(inc, incn)
        else:
            tgt = None
        wl = [(self.sem(n), v) for (n, v) in waits if v > 0]
        ih = self.semh[inc] if inc is not None else None

        def run(eng, build=build, wl=wl, ih=ih, incn=incn):
            for (h, v) in wl:
                eng.wait_ge(h, v)
            inst = build(eng)
            if ih is not None:
                inst.then_inc(ih, incn)

        self.streams[stream].append(run)
        return tgt

    def dma(self, stream, out, in_, inc, waits=(), slot=None):
        name = inc if slot is None else "%s.%d" % (inc, slot)
        v = self.emit(stream, lambda e, o=out, i=in_: e.dma_start(out=o, in_=i),
                      inc=name, waits=waits, incn=16)
        return (name, v)

    def dma_totals(self, base):
        return [(n, c) for n, c in self.semc.items()
                if (n == base or n.startswith(base + ".")) and c > 0]

    def wait_only(self, stream, waits):
        wl = [(self.sem(n), v) for (n, v) in waits if v > 0]

        def run(eng, wl=wl):
            for (h, v) in wl:
                eng.wait_ge(h, v)
        self.streams[stream].append(run)

    def run_phase(self):
        with self.nc.Block() as blk:
            for name in ENGINES:
                fns = self.streams[name]
                if not fns:
                    continue
                getattr(blk, name)(lambda eng, fns=tuple(fns): [f(eng) for f in fns])
                self.streams[name] = []



def emit_ln_tile(p, xt, xn_bf, tmp, stat, tag):
    sums, sumsq, mean, var, rstd, lnv = (stat[:, i:i + 1] for i in range(6))
    v0 = p.emit("vector", lambda v: v.reduce_sum(out=sums, in_=xt, axis=AX.X), inc=tag)
    v1 = p.emit("vector", lambda v: v.tensor_mul(out=tmp, in0=xt, in1=xt), inc=tag)
    v2 = p.emit("vector", lambda v: v.reduce_sum(out=sumsq, in_=tmp, axis=AX.X),
                inc=tag, waits=[(tag, v1)])
    v3 = p.emit("vector", lambda v: v.tensor_single_scalar(out=mean, in_=sums,
                scalar=1.0 / D, op=ALU.mult), inc=tag, waits=[(tag, v0)])
    v4 = p.emit("vector", lambda v: v.tensor_scalar(out=var, in0=mean, scalar1=mean,
                scalar2=1e-5, op0=ALU.mult, op1=ALU.subtract), inc=tag, waits=[(tag, v3)])
    v5 = p.emit("vector", lambda v: v.scalar_tensor_tensor(out=var, in0=sumsq,
                scalar=1.0 / D, in1=var, op0=ALU.mult, op1=ALU.subtract),
                inc=tag, waits=[(tag, max(v2, v4))])
    a1 = p.emit("scalar", lambda sc: sc.activation(out=lnv, in_=var, func=AF.Ln),
                inc=tag, waits=[(tag, v5)])
    a2 = p.emit("scalar", lambda sc: sc.activation(out=rstd, in_=lnv, func=AF.Exp,
                scale=-0.5), inc=tag, waits=[(tag, a1)])
    v6 = p.emit("vector", lambda v: v.tensor_scalar(out=xn_bf, in0=xt, scalar1=mean,
                scalar2=rstd, op0=ALU.subtract, op1=ALU.mult),
                inc=tag, waits=[(tag, a2)])
    return v6


def emit_transpose_chunks(p, srcs_dsts, ident, pst, mmtag, evtag,
                          first_mm_waits=(), first_evac_waits=(), evac="scalar"):
    mv = ev = None
    for i, (src, dst) in enumerate(srcs_dsts):
        waits = list(first_mm_waits) if i == 0 else []
        waits.append((evtag, p.val(evtag) if ev is None else ev))
        mv = p.emit("tensor", lambda t, src=src: t.transpose(
            out=pst, in_=src, identity=ident), inc=mmtag, waits=waits)
        ew = [(mmtag, mv)]
        if i == 0:
            ew += list(first_evac_waits)
        ev = p.emit(evac, lambda e, dst=dst: (
            e.tensor_copy(out=dst, in_=pst) if hasattr(e, "tensor_copy")
            else e.copy(out=dst, in_=pst)), inc=evtag, waits=ew)
    return mv, ev


def emit_transpose_groups(p, groups, ident, pst4, mmtag, evtag,
                          first_mm_waits=(), first_evac_waits=(), evac="scalar"):
    mv = ev = None
    for gi, (srcs, dst) in enumerate(groups):
        for si, srcap in enumerate(srcs):
            waits = list(first_mm_waits) if gi == 0 and si == 0 else []
            if si == 0:
                waits.append((evtag, p.val(evtag) if ev is None else ev))
            mv = p.emit("tensor", lambda t, srcap=srcap, si=si: t.transpose(
                out=pst4[:, si * 128:(si + 1) * 128], in_=srcap, identity=ident),
                inc=mmtag, waits=waits)
        ew = [(mmtag, mv)]
        if gi == 0:
            ew += list(first_evac_waits)
        n = len(srcs)
        ev = p.emit(evac, lambda e, dst=dst, n=n: (
            e.tensor_copy(out=dst, in_=pst4[:, 0:n * 128])
            if hasattr(e, "tensor_copy")
            else e.copy(out=dst, in_=pst4[:, 0:n * 128])), inc=evtag, waits=ew)
    return mv, ev



def emit_l1(p, g, io, debug=()):
    nc = p.nc
    x1, qkvw, qkvb, outw, outb, wfc, wproj, yout = (
        io["x1"], io["qkvw"], io["qkvb"], io["outw"], io["outb"],
        io["wfc"], io["wproj"], io["y"])
    g1_dram = io["g1_dram"]
    ident_t = io["ident_t"]
    nc0 = p.nc
    wfcT_d = nc0.dram_tensor("wfcT_d", [D, H1], BF16, kind="Internal")
    wprojT_d = nc0.dram_tensor("wprojT_d", [H1, D], BF16, kind="Internal")

    xn1T = g.enter_context(nc.sbuf_tensor("xn1T", [128, NCH * RPC], BF16))
    mu_sb = g.enter_context(nc.sbuf_tensor("mu_sb", [128, 8], F32))
    ident = g.enter_context(nc.sbuf_tensor("ident", [128, 128], BF16))
    ones_bf = g.enter_context(nc.sbuf_tensor("ones_bf", [1, 128], BF16))
    handles = {"xn1T": xn1T, "mu_sb": mu_sb}

    with ExitStack() as ph:
        wtile = ph.enter_context(nc.sbuf_tensor("p0_wtile", [128, 2 * D], F32))
        wtile2 = ph.enter_context(nc.sbuf_tensor("p0_wtile2", [128, 2 * H1], F32))
        acc = ph.enter_context(nc.sbuf_tensor("p0_acc", [128, 2], F32))
        red = ph.enter_context(nc.sbuf_tensor("p0_red", [128, 2], F32))
        onesf = ph.enter_context(nc.sbuf_tensor("p0_onesf", [128, 1], F32))
        onesr = ph.enter_context(nc.sbuf_tensor("p0_onesr", [1, 128], F32))
        muvec = ph.enter_context(nc.sbuf_tensor("p0_muvec", [1, 8], F32))
        sumv = ph.enter_context(nc.sbuf_tensor("p0_sumv", [1, 2], F32))
        invn = ph.enter_context(nc.sbuf_tensor("p0_invn", [1, 2], F32))
        ps_mu = ph.enter_context(nc.psum_tensor("p0_psmu", [1, 2], F32))
        ps_bc = ph.enter_context(nc.psum_tensor("p0_psbc", [128, 8], F32))

        NW_PJ = D // 128

        ident_ld = p.dma("sync", ident[:, :], ident_t.ap(), "ld0")
        m0 = p.emit("vector", lambda v: v.memset(onesf[:, :], 1.0), inc="p0v")
        p.emit("vector", lambda v: v.memset(onesr[:, :], 1.0), inc="p0v")
        p.emit("vector", lambda v: v.memset(ones_bf[:, :], 1.0), inc="p0v")
        p.emit("vector", lambda v: v.memset(acc[:, :], 0.0), inc="p0v")
        p.emit("vector", lambda v: v.memset(invn[:, 0:1], 1.0 / (H1 * D)), inc="p0v")
        minit = p.emit("vector", lambda v: v.memset(invn[:, 1:2], 1.0 / (H1 * D)),
                       inc="p0v")

        for t in range(NFC):
            buf = wtile[:, (t % 2) * D:(t % 2) * D + D]
            waits = [("mured", 2 * (t - 1))] if t >= 2 else []
            lv = p.dma("sync", buf, wfc[t * 128:(t + 1) * 128, :], "ldw",
                       waits=waits, slot=t % 2)
            rv = p.emit("vector", lambda v, buf=buf: v.tensor_reduce(
                out=red[:, 0:1], in_=buf, axis=AX.X, op=ALU.add,
                apply_absolute_value=True), inc="mured",
                waits=[lv, ("mured", p.val("mured")), ("p0v", minit)])
            p.emit("vector", lambda v: v.tensor_add(out=acc[:, 0:1], in0=acc[:, 0:1],
                   in1=red[:, 0:1]), inc="mured", waits=[("mured", rv)])
        for t in range(NW_PJ):
            buf = wtile2[:, (t % 2) * H1:(t % 2) * H1 + H1]
            waits = [("mured", 2 * NFC + 2 * (t - 1))] if t >= 2 else []
            lv = p.dma("sync", buf, wproj[t * 128:(t + 1) * 128, :], "ldw",
                       waits=waits, slot=t % 2)
            rv = p.emit("vector", lambda v, buf=buf: v.tensor_reduce(
                out=red[:, 1:2], in_=buf, axis=AX.X, op=ALU.add,
                apply_absolute_value=True), inc="mured",
                waits=[lv, ("mured", p.val("mured"))])
            p.emit("vector", lambda v: v.tensor_add(out=acc[:, 1:2], in0=acc[:, 1:2],
                   in1=red[:, 1:2]), inc="mured", waits=[("mured", rv)])

        mm1 = p.emit("tensor", lambda t: t.matmul(out=ps_mu[:, :], lhsT=onesf[:, :],
                     rhs=acc[:, :], start=True, stop=True), inc="p0mm",
                     waits=[("mured", p.val("mured"))])
        w1 = p.emit("vector", lambda v: v.tensor_mul(out=sumv[:, :], in0=ps_mu[0:1, :],
                    in1=invn[:, :]), inc="p0v", waits=[("p0mm", mm1)])
        w2 = p.emit("vector", lambda v: v.tensor_scalar_max(out=sumv[:, :],
                    in0=sumv[:, :], scalar1=1e-5), inc="p0v", waits=[("p0v", w1)])
        last = w2
        for scal, col in [(0.7, 0), (-0.7, 1), (-1.0, 2)]:
            p.emit("vector", lambda v, scal=scal, col=col: v.tensor_single_scalar(
                out=muvec[:, col:col + 1], in_=sumv[:, 0:1], scalar=scal, op=ALU.mult),
                inc="p0v", waits=[("p0v", w2)])
            last = p.emit("vector", lambda v, scal=scal, col=col: v.tensor_single_scalar(
                out=muvec[:, 3 + col:4 + col], in_=sumv[:, 1:2], scalar=scal,
                op=ALU.mult), inc="p0v", waits=[("p0v", w2)])
        z = p.emit("vector", lambda v: v.memset(muvec[:, 6:8], 0.0), inc="p0v",
                   waits=[("p0v", last)])
        mm2 = p.emit("tensor", lambda t: t.matmul(out=ps_bc[:, :], lhsT=onesr[:, :],
                     rhs=muvec[:, :], start=True, stop=True), inc="p0mm",
                     waits=[("p0v", z)])
        p.emit("vector", lambda v: v.tensor_copy(out=mu_sb[:, :], in_=ps_bc[:, :]),
               inc="p0done", waits=[("p0mm", mm2)])
        p.wait_only("sync", [ident_ld, ("p0done", 1)])
        p.run_phase()

    with ExitStack() as ph:
        x1t = ph.enter_context(nc.sbuf_tensor("p1_x1t", [128, 2 * D], F32))
        xnbf = ph.enter_context(nc.sbuf_tensor("p1_xnbf", [128, 2 * D], BF16))
        tmp = ph.enter_context(nc.sbuf_tensor("p1_tmp", [128, D], F32))
        stat = ph.enter_context(nc.sbuf_tensor("p1_stat", [128, 8], F32))
        pst = ph.enter_context(nc.psum_tensor("p1_pst", [128, 512], BF16))

        xn1T_v = xn1T[:, :].rearrange("p (c r) -> p c r", c=NCH)
        ln_rdy = []
        mm_cnt = []
        for t in range(NT):
            xb = x1t[:, (t % 2) * D:(t % 2) * D + D]
            nb = xnbf[:, (t % 2) * D:(t % 2) * D + D]
            waits = [("p1ln", ln_rdy[t - 2])] if t >= 2 else []
            lv = p.dma("sync", xb, x1[t * 128:(t + 1) * 128, :], "ldx",
                       waits=waits, slot=t % 2)
            p.wait_only("vector", [lv])
            if t >= 2:
                p.wait_only("vector", [("p1mm", mm_cnt[t - 2])])
            rdy = emit_ln_tile(p, xb, nb, tmp[:, :], stat, "p1ln")
            ln_rdy.append(rdy)
            groups = [([nb[:, ci * 128:(ci + 1) * 128] for ci in range(c0, c0 + 4)],
                       xn1T_v[:, c0:c0 + 4, t * 128:(t + 1) * 128])
                      for c0 in range(0, NCH, 4)]
            mv, ev = emit_transpose_groups(p, groups, ident[:, :], pst[:, :],
                                           "p1mm", "p1ev",
                                           first_mm_waits=[("p1ln", rdy)])
            mm_cnt.append(mv)
        p.run_phase()

    with ExitStack() as ph:
        wld = ph.enter_context(nc.sbuf_tensor("p2_wld", [128, 2 * 2560], F32))
        wtn = ph.enter_context(nc.sbuf_tensor("p2_wtn", [128, 2560], F32))
        wtnb = ph.enter_context(nc.sbuf_tensor("p2_wtnb", [128, 2560], BF16))
        wT = ph.enter_context(nc.sbuf_tensor("p2_wT", [128, 2 * H1], BF16))
        hT = ph.enter_context(nc.sbuf_tensor("p2_hT", [128, NFC * 1024], BF16))
        relu_t = ph.enter_context(nc.sbuf_tensor("p2_relu", [128, 512], F32))
        sil_t = ph.enter_context(nc.sbuf_tensor("p2_sil", [128, 512], F32))
        g1sil = ph.enter_context(nc.sbuf_tensor("p2_g1sil", [128, 2 * 512], BF16))
        g1stg = ph.enter_context(nc.sbuf_tensor("p2_g1stg", [128, 2 * 128], BF16))
        pst = ph.enter_context(nc.psum_tensor("p2_pst", [128, 512], BF16))
        psh = [ph.enter_context(nc.psum_tensor(f"p2_psh{i}", [128, 512], F32))
               for i in range(2)]

        thr_fc, nthr_fc, nmu_fc = mu_sb[:, 0:1], mu_sb[:, 1:2], mu_sb[:, 2:3]
        thr_pj, nthr_pj, nmu_pj = mu_sb[:, 3:4], mu_sb[:, 4:5], mu_sb[:, 5:6]
        g1st_hist = [None, None]
        wtd_hist = [None, None]
        hev_slot = [0, 0]
        gev_slot = [0, 0]

        def tern_halftile(src, width, thr, nthr, nmu, dst_bf, extra_waits):
            a = wtn[:, 0:width]
            t1 = p.emit("vector", lambda v: v.tensor_single_scalar(
                out=a, in_=src, scalar=thr, op=ALU.is_gt),
                inc="tern", waits=list(extra_waits) + [("tern", p.val("tern"))])
            t2 = p.emit("vector", lambda v: v.scalar_tensor_tensor(
                out=a, in0=src, scalar=nthr, in1=a, op0=ALU.is_lt, op1=ALU.subtract),
                inc="tern", waits=[("tern", t1)])
            t3 = p.emit("vector", lambda v: v.tensor_single_scalar(
                out=dst_bf, in_=a, scalar=nmu, op=ALU.mult),
                inc="tern", waits=[("tern", t2), ("p2mm", p.val("p2mm"))])
            return t3

        for RG in range(2):
            r0 = RG * 1024
            from concourse.ap import AP as _AP
            for ns in range(NFC):
                wTb = wT[:, (ns % 2) * H1:(ns % 2) * H1 + 2048]
                evs = []
                if RG == 0:
                    for half in range(2):
                        buf = wld[:, (half % 2) * 2560:(half % 2) * 2560 + 1024]
                        lv = p.dma("sync", buf,
                                   wfc[ns * 128:(ns + 1) * 128, half * 1024:half * 1024 + 1024],
                                   "ldw2", waits=[("tern", p.val("tern"))], slot=half)
                        tv = tern_halftile(buf, 1024, thr_fc, nthr_fc, nmu_fc,
                                           wtnb[:, 0:1024], [lv])
                        groups = [([wtnb[:, ci * 128:(ci + 1) * 128]
                                    for ci in range(c0, c0 + 4)],
                                   wTb[:, (half * 8 + c0) * 128:(half * 8 + c0 + 4) * 128])
                                  for c0 in range(0, 8, 4)]
                        _, ev = emit_transpose_groups(
                            p, groups, ident[:, :], pst[:, :], "p2mm", "wtev",
                            first_mm_waits=[("tern", tv)],
                            first_evac_waits=[("hmm", p.val("hmm"))] + (
                                [wtd_hist[ns % 2]] if wtd_hist[ns % 2] else []),
                            evac="vector")
                        evs.append(ev)
                    dst = _AP(wfcT_d.ap().tensor, ns * 128,
                              [[H1, 128], [128 * H1, NCH], [1, 128]])
                    wtd_hist[ns % 2] = p.dma(
                        "gpsimd", dst, wTb.rearrange("p (c j) -> p c j", c=NCH),
                        "wtd", waits=[("wtev", evs[-1])], slot=ns % 2)
                else:
                    srcap = _AP(wfcT_d.ap().tensor, ns * 128,
                                [[H1, 128], [128 * H1, NCH], [1, 128]])
                    lvw = p.dma("sync", wTb.rearrange("p (c j) -> p c j", c=NCH),
                                srcap, "wtl",
                                waits=[("hmm", max(0, p.val("hmm") - 32))],
                                slot=ns % 2)
                    evs = [None]
                for sc_i in range(2):
                    ps = psh[sc_i % 2]
                    mmv = None
                    for ci in range(NCH):
                        w = []
                        if ci == 0:
                            w = ([("wtev", evs[-1])] if evs[-1] else [lvw]) + \
                                [("hev", p.val("hev"))]
                        mmv = p.emit("tensor", lambda t, ps=ps, ci=ci, wTb=wTb, sc_i=sc_i, r0=r0: t.matmul(
                            out=ps[:, :], lhsT=wTb[:, ci * 128:ci * 128 + 128],
                            rhs=xn1T[:, ci * RPC + r0 + sc_i * 512:
                                     ci * RPC + r0 + sc_i * 512 + 512],
                            start=(ci == 0), stop=(ci == NCH - 1)), inc="hmm", waits=w)
                    u1 = p.emit("vector", lambda v, ps=ps: v.tensor_scalar_max(
                        out=relu_t[:, :], in0=ps[:, :], scalar1=0.0), inc="hev",
                        waits=[("hmm", mmv), ("hev", p.val("hev"))])
                    hev_slot[sc_i % 2] = u1
                    p.emit("vector", lambda v, ns=ns, sc_i=sc_i: v.tensor_mul(
                        out=hT[:, ns * 1024 + sc_i * 512: ns * 1024 + sc_i * 512 + 512],
                        in0=relu_t[:, :], in1=relu_t[:, :]), inc="hev",
                        waits=[("hev", u1), ("gmm", p.val("gmm"))])
            hT_done = p.val("hev")
            for ocs in range(NCH):
                wTb = wT[:, (ocs % 2) * H1:(ocs % 2) * H1 + H1]
                evs = []
                if RG == 0:
                    for half in range(2):
                        buf = wld[:, (half % 2) * 2560:(half % 2) * 2560 + 2560]
                        lv = p.dma("sync", buf,
                                   wproj[ocs * 128:(ocs + 1) * 128,
                                         half * 2560:half * 2560 + 2560],
                                   "ldw2", waits=[("tern", p.val("tern"))], slot=half)
                        tv = tern_halftile(buf, 2560, thr_pj, nthr_pj, nmu_pj,
                                           wtnb[:, 0:2560], [lv])
                        groups = [([wtnb[:, ch * 128:(ch + 1) * 128]
                                    for ch in range(c0, min(c0 + 4, 20))],
                                   wTb[:, (half * 20 + c0) * 128:
                                       (half * 20 + min(c0 + 4, 20)) * 128])
                                  for c0 in range(0, 20, 4)]
                        _, ev = emit_transpose_groups(
                            p, groups, ident[:, :], pst[:, :], "p2mm", "wtev",
                            first_mm_waits=[("tern", tv)],
                            first_evac_waits=[("gmm", p.val("gmm"))] + (
                                [wtd_hist[ocs % 2]] if wtd_hist[ocs % 2] else []),
                            evac="vector")
                        evs.append(ev)
                    dst = _AP(wprojT_d.ap().tensor, ocs * 128,
                              [[D, 128], [128 * D, NFC], [1, 128]])
                    wtd_hist[ocs % 2] = p.dma(
                        "gpsimd", dst, wTb.rearrange("p (c j) -> p c j", c=NFC),
                        "wtd", waits=[("wtev", evs[-1])], slot=ocs % 2)
                else:
                    srcap = _AP(wprojT_d.ap().tensor, ocs * 128,
                                [[D, 128], [128 * D, NFC], [1, 128]])
                    lvw = p.dma("sync", wTb.rearrange("p (c j) -> p c j", c=NFC),
                                srcap, "wtl",
                                waits=[("gmm", max(0, p.val("gmm") - 80))],
                                slot=ocs % 2)
                    evs = [None]
                for sc_i in range(2):
                    ps = psh[sc_i % 2]
                    mmv = None
                    for ch in range(NFC):
                        w = []
                        if ch == 0:
                            w = ([("wtev", evs[-1])] if evs[-1] else [lvw]) + \
                                [("gev", p.val("gev")), ("hev", hT_done)]
                        mmv = p.emit("tensor", lambda t, ps=ps, ch=ch, wTb=wTb, sc_i=sc_i: t.matmul(
                            out=ps[:, :], lhsT=wTb[:, ch * 128:ch * 128 + 128],
                            rhs=hT[:, ch * 1024 + sc_i * 512: ch * 1024 + sc_i * 512 + 512],
                            start=(ch == 0), stop=(ch == NFC - 1)), inc="gmm", waits=w)
                    s1 = p.emit("scalar", lambda sc, ps=ps: sc.activation(
                        out=sil_t[:, :], in_=ps[:, :], func=AF.Tanh, scale=0.5),
                        inc="gev", waits=[("gmm", mmv), ("gev", p.val("gev"))])
                    s2 = p.emit("vector", lambda v: v.tensor_scalar(
                        out=sil_t[:, :], in0=sil_t[:, :], scalar1=0.5, scalar2=0.5,
                        op0=ALU.mult, op1=ALU.add), inc="gev", waits=[("gev", s1)])
                    gb = g1sil[:, (sc_i % 2) * 512:(sc_i % 2) * 512 + 512]
                    s3 = p.emit("vector", lambda v, ps=ps, gb=gb: v.tensor_mul(
                        out=gb, in0=ps[:, :], in1=sil_t[:, :]), inc="gev",
                        waits=[("gev", s2), ("p2mm", p.val("p2mm"))])
                    gev_slot[sc_i % 2] = s3
                    for rt in range(4):
                        stg = g1stg[:, (rt % 2) * 128:(rt % 2) * 128 + 128]
                        _, ev = emit_transpose_chunks(
                            p, [(gb[:, rt * 128:rt * 128 + 128], stg)],
                            ident[:, :], pst[:, 0:128], "p2mm", "wtev",
                            first_mm_waits=[("gev", s3)],
                            first_evac_waits=(
                                [g1st_hist[rt % 2]] if g1st_hist[rt % 2] else []),
                            evac="vector")
                        row0 = r0 + sc_i * 512 + rt * 128
                        g1st_hist[rt % 2] = p.dma("gpsimd",
                              g1_dram.ap()[row0:row0 + 128, ocs * 128:ocs * 128 + 128],
                              stg, "g1st", waits=[("wtev", ev)], slot=rt % 2)
        p.wait_only("gpsimd", p.dma_totals("g1st"))
        p.run_phase()

    attnT = g.enter_context(nc.sbuf_tensor("attnT", [128, NCH * RPC], BF16))
    handles["attnT"] = attnT
    with ExitStack() as ph:
        qkvb_sb = ph.enter_context(nc.sbuf_tensor("qkvb_sb", [1, 3072], BF16))
        qkv_sb = ph.enter_context(nc.sbuf_tensor("p3_qkv", [128, 4 * 3072], BF16))
        wqt = ph.enter_context(nc.sbuf_tensor("p3_wqt", [128, 2 * NCH * 512], BF16))
        at_st = ph.enter_context(nc.sbuf_tensor("p3_atst", [128, 2 * D], BF16))
        sprod = ph.enter_context(nc.sbuf_tensor("p3_sprod", [128, D], BF16))
        sdot = ph.enter_context(nc.sbuf_tensor("p3_sdot", [128, 48], F32))
        dv_t = ph.enter_context(nc.sbuf_tensor("p3_dv", [128, 512], F32))
        alph = ph.enter_context(nc.sbuf_tensor("p3_alph", [128, 16], F32))
        tanh_t = ph.enter_context(nc.sbuf_tensor("p3_tanh", [128, 16], F32))
        pst = ph.enter_context(nc.psum_tensor("p3_pst", [128, 512], BF16))
        psq = [ph.enter_context(nc.psum_tensor(f"p3_psq{i}", [128, 512], F32))
               for i in range(4)]

        qkvb_ld = p.dma("gpsimd", qkvb_sb[:, :], qkvb, "ld0")
        qev_slot = [0, 0, 0, 0]
        for grp in range(4):
            taus = [2 * grp, 2 * grp + 1, 8 + 2 * grp, 8 + 2 * grp + 1]
            for ng in range(6):
                mml = [None] * 4
                par = (grp * 6 + ng) % 2
                wball = wqt[:, par * NCH * 512:(par + 1) * NCH * 512]
                w = [("qmm", max(0, p.val("qmm") - 68))]
                lv = p.dma("gpsimd", wball.rearrange("p (c n) -> p c n", c=NCH),
                           qkvw[:, ng * 512:ng * 512 + 512]
                           .rearrange("(c p) n -> p c n", p=128),
                           "qwl", waits=w, slot=par)
                for ci in range(NCH):
                    wb = wball[:, ci * 512:ci * 512 + 512]
                    for ti in range(4):
                        tau = taus[ti]
                        w2 = [lv] if ti == 0 and ci == 0 else []
                        if ci == 0:
                            w2.append(("qev", p.val("qev")))
                        mml[ti] = p.emit("tensor", lambda t, ti=ti, ci=ci, tau=tau, wb=wb: t.matmul(
                            out=psq[ti][:, :],
                            lhsT=xn1T[:, ci * RPC + tau * 128: ci * RPC + tau * 128 + 128],
                            rhs=wb, start=(ci == 0), stop=False), inc="qmm", waits=w2)
                for ti in range(4):
                    mml[ti] = p.emit("tensor", lambda t, ti=ti, ng=ng: t.matmul(
                        out=psq[ti][:, :], lhsT=ones_bf[:, :],
                        rhs=qkvb_sb[:, ng * 512:ng * 512 + 512],
                        start=False, stop=True), inc="qmm",
                        waits=[qkvb_ld] if grp == 0 and ng == 0 else [])
                    qev_slot[ti] = p.emit("vector", lambda v, ti=ti, ng=ng: v.tensor_copy(
                        out=qkv_sb[:, ti * 3072 + ng * 512: ti * 3072 + ng * 512 + 512],
                        in_=psq[ti][:, :]), inc="qev", waits=[("qmm", mml[ti])])
            qkv_rdy = p.val("qev")
            for pr in range(2):
                i0, i1 = pr, 2 + pr
                tau0, tau1 = taus[i0], taus[i1]
                q1 = qkv_sb[:, i1 * 3072: i1 * 3072 + D]
                k0 = qkv_sb[:, i0 * 3072 + D: i0 * 3072 + D + 512]
                k1 = qkv_sb[:, i1 * 3072 + D: i1 * 3072 + D + 512]
                v0 = qkv_sb[:, i0 * 3072 + D + 512: i0 * 3072 + 3072]
                v1 = qkv_sb[:, i1 * 3072 + D + 512: i1 * 3072 + 3072]
                q1v = q1.rearrange("p (g r d) -> p g r d", g=4, r=4)
                sp4 = sprod[:, :].rearrange("p (g r d) -> p g r d", g=4, r=4)
                sph = sprod[:, :].rearrange("p (h d) -> p h d", h=16)
                stv = []
                for tix, kt in enumerate((k0, k1)):
                    ktv = kt.rearrange("p (g d) -> p g d", g=4).unsqueeze(2) \
                        .broadcast_to([128, 4, 4, HDIM])
                    m1 = p.emit("vector", lambda v, q1v=q1v, ktv=ktv, sp4=sp4: v.tensor_tensor(
                        out=sp4, in0=q1v, in1=ktv, op=ALU.mult),
                        inc="att", waits=[("qev", qkv_rdy), ("att", p.val("att"))])
                    m2 = p.emit("vector", lambda v, tix=tix, sph=sph: v.tensor_reduce(
                        out=sdot[:, tix * 16:tix * 16 + 16], in_=sph,
                        axis=AX.X, op=ALU.add), inc="att", waits=[("att", m1)])
                    stv.append(m2)
                d1 = p.emit("vector", lambda v: v.tensor_sub(
                    out=sdot[:, 32:48], in0=sdot[:, 16:32], in1=sdot[:, 0:16]),
                    inc="att", waits=[("att", max(stv))])
                d2 = p.emit("vector", lambda v: v.tensor_single_scalar(
                    out=sdot[:, 32:48], in_=sdot[:, 32:48],
                    scalar=0.5 * INVSQRTD, op=ALU.mult), inc="att", waits=[("att", d1)])
                d3 = p.emit("scalar", lambda sc: sc.activation(
                    out=tanh_t[:, :], in_=sdot[:, 32:48], func=AF.Tanh),
                    inc="att", waits=[("att", d2)])
                d4 = p.emit("vector", lambda v: v.tensor_scalar(
                    out=alph[:, :], in0=tanh_t[:, :], scalar1=0.5, scalar2=0.5,
                    op0=ALU.mult, op1=ALU.add), inc="att", waits=[("att", d3)])
                v0b = v0.rearrange("p (g d) -> p g d", g=4).unsqueeze(2) \
                    .broadcast_to([128, 4, 4, HDIM])
                at0 = at_st[:, 0:D].rearrange("p (g r d) -> p g r d", g=4, r=4)
                at1 = at_st[:, D:2 * D].rearrange("p (g r d) -> p g r d", g=4, r=4)
                a0 = p.emit("vector", lambda v, v0b=v0b, at0=at0: v.tensor_copy(
                    out=at0, in_=v0b), inc="att",
                    waits=[("qev", qkv_rdy), ("p3mm", p.val("p3mm"))])
                b1 = p.emit("vector", lambda v, v0=v0, v1=v1: v.tensor_sub(
                    out=dv_t[:, :], in0=v1, in1=v0), inc="att",
                    waits=[("qev", qkv_rdy)])
                dvb = dv_t[:, :].rearrange("p (g d) -> p g d", g=4).unsqueeze(2) \
                    .broadcast_to([128, 4, 4, HDIM])
                alb = alph[:, :].rearrange("p (g r) -> p g r", g=4).unsqueeze(3) \
                    .broadcast_to([128, 4, 4, HDIM])
                b2 = p.emit("vector", lambda v, dvb=dvb, alb=alb, sp4=sp4: v.tensor_tensor(
                    out=sp4, in0=dvb, in1=alb, op=ALU.mult), inc="att",
                    waits=[("att", max(d4, b1))])
                b3 = p.emit("vector", lambda v, v0b=v0b, sp4=sp4, at1=at1: v.tensor_tensor(
                    out=at1, in0=sp4, in1=v0b, op=ALU.add), inc="att",
                    waits=[("att", b2), ("p3mm", p.val("p3mm"))])
                attnT_v = attnT[:, :].rearrange("p (c r) -> p c r", c=NCH)
                for which, (tau, rdy) in enumerate([(tau0, a0), (tau1, b3)]):
                    src = at_st[:, which * D:which * D + D]
                    groups = [([src[:, ci * 128:(ci + 1) * 128]
                                for ci in range(c0, c0 + 4)],
                               attnT_v[:, c0:c0 + 4, tau * 128:(tau + 1) * 128])
                              for c0 in range(0, NCH, 4)]
                    emit_transpose_groups(p, groups, ident[:, :], pst[:, :],
                                          "p3mm", "p3ev",
                                          first_mm_waits=[("att", rdy)],
                                          evac="scalar")
        p.run_phase()

    with ExitStack() as ph:
        outb_sb = ph.enter_context(nc.sbuf_tensor("outb_sb", [1, D], BF16))
        wot = ph.enter_context(nc.sbuf_tensor("p4_wot", [128, 2 * NCH * 512], BF16))
        x1c = ph.enter_context(nc.sbuf_tensor("p4_x1c", [128, 2 * 512], F32))
        g1c = ph.enter_context(nc.sbuf_tensor("p4_g1c", [128, 2 * 512], BF16))
        outst = ph.enter_context(nc.sbuf_tensor("p4_out", [128, 2 * 512], F32))
        psw = [ph.enter_context(nc.psum_tensor(f"p4_psw{i}", [128, 512], F32))
               for i in range(8)]

        outb_ld = p.dma("gpsimd", outb_sb[:, :], outb, "ld0")
        fst_hist = [None, None]
        fin_slot = [0] * 8
        for grp in range(2):
            taus = list(range(grp * 8, grp * 8 + 8))
            for ng in range(4):
                mml = [None] * 8
                par = (grp * 4 + ng) % 2
                wball = wot[:, par * NCH * 512:(par + 1) * NCH * 512]
                w = [("omm", max(0, p.val("omm") - 136))]
                lv = p.dma("gpsimd", wball.rearrange("p (c n) -> p c n", c=NCH),
                           outw[:, ng * 512:ng * 512 + 512]
                           .rearrange("(c p) n -> p c n", p=128),
                           "owl", waits=w, slot=par)
                for ci in range(NCH):
                    wb = wball[:, ci * 512:ci * 512 + 512]
                    for ti in range(8):
                        tau = taus[ti]
                        w2 = [lv] if ti == 0 and ci == 0 else []
                        if ci == 0:
                            w2.append(("fin", p.val("fin")))
                        mml[ti] = p.emit("tensor", lambda t, ti=ti, ci=ci, tau=tau, wb=wb: t.matmul(
                            out=psw[ti][:, :],
                            lhsT=attnT[:, ci * RPC + tau * 128: ci * RPC + tau * 128 + 128],
                            rhs=wb, start=(ci == 0), stop=False), inc="omm", waits=w2)
                for ti in range(8):
                    tau = taus[ti]
                    mml[ti] = p.emit("tensor", lambda t, ti=ti, ng=ng: t.matmul(
                        out=psw[ti][:, :], lhsT=ones_bf[:, :],
                        rhs=outb_sb[:, ng * 512:ng * 512 + 512],
                        start=False, stop=True), inc="omm",
                        waits=[outb_ld] if grp == 0 and ng == 0 else [])
                    par = ti % 2
                    xb = x1c[:, par * 512:par * 512 + 512]
                    gb = g1c[:, par * 512:par * 512 + 512]
                    ob = outst[:, par * 512:par * 512 + 512]
                    wv = [("fin", p.val("fin"))] if ti >= 2 or ng > 0 or grp > 0 else []
                    lx = p.dma("sync", xb,
                               x1[tau * 128:(tau + 1) * 128, ng * 512:ng * 512 + 512],
                               "fxlx", waits=wv, slot=par)
                    lg = p.dma("sync", gb,
                               g1_dram.ap()[tau * 128:(tau + 1) * 128,
                                            ng * 512:ng * 512 + 512], "fxlg", slot=par)
                    f1 = p.emit("vector", lambda v, ti=ti, gb=gb, ob=ob: v.tensor_tensor(
                        out=ob, in0=psw[ti][:, :], in1=gb, op=ALU.mult), inc="fin",
                        waits=[("omm", mml[ti]), lg] +
                              ([fst_hist[par]] if fst_hist[par] else []))
                    fin_slot[ti] = f1
                    f2 = p.emit("vector", lambda v, ob=ob, xb=xb: v.tensor_add(
                        out=ob, in0=ob, in1=xb), inc="fin",
                        waits=[("fin", f1), lx])
                    fst_hist[par] = p.dma("gpsimd",
                          yout[tau * 128:(tau + 1) * 128, ng * 512:ng * 512 + 512],
                          ob, "fst", waits=[("fin", f2)], slot=par)
        p.wait_only("gpsimd", p.dma_totals("fst"))
        p.run_phase()

    return handles





RGALL = [[i for i in range(NCORES)]]


def emit_dequant_tile(p, wt, sc, out_bf, bufs, kp, tag, out_war_waits=()):
    mi, si, mf, val, t1, t2, t3, sgn = (bufs[k] for k in
                                       ("mi", "si", "mf", "val", "t1", "t2", "t3", "sgn"))
    base = p.val(tag)
    p.emit("vector", lambda v: v.tensor_single_scalar(
        out=mi[:, 0:kp], in_=wt, scalar=7, op=ALU.bitwise_and), inc=tag,
        waits=[(tag, base)])
    e2 = p.emit("vector", lambda v: v.tensor_scalar(
        out=mi[:, kp:2 * kp], in0=wt, scalar1=4, scalar2=7,
        op0=ALU.logical_shift_right, op1=ALU.bitwise_and), inc=tag)
    e3 = p.emit("vector", lambda v: v.tensor_scalar(
        out=si[:, 0:kp], in0=wt, scalar1=3, scalar2=1,
        op0=ALU.logical_shift_right, op1=ALU.bitwise_and), inc=tag)
    e4 = p.emit("vector", lambda v: v.tensor_scalar(
        out=si[:, kp:2 * kp], in0=wt, scalar1=7, scalar2=1,
        op0=ALU.logical_shift_right, op1=ALU.bitwise_and), inc=tag)
    e5 = p.emit("vector", lambda v: v.tensor_copy(out=mf[:, 0:2 * kp], in_=mi[:, 0:2 * kp]),
                inc=tag, waits=[(tag, e2)])
    e6 = p.emit("vector", lambda v: v.tensor_single_scalar(
        out=t1[:, 0:2 * kp], in_=mf[:, 0:2 * kp], scalar=4.5, op=ALU.is_ge), inc=tag,
        waits=[(tag, e5)])
    e7 = p.emit("vector", lambda v: v.tensor_single_scalar(
        out=t2[:, 0:2 * kp], in_=mf[:, 0:2 * kp], scalar=5.5, op=ALU.is_ge), inc=tag)
    e8 = p.emit("vector", lambda v: v.tensor_single_scalar(
        out=t3[:, 0:2 * kp], in_=mf[:, 0:2 * kp], scalar=6.5, op=ALU.is_ge), inc=tag)
    e9 = p.emit("vector", lambda v: v.tensor_single_scalar(
        out=val[:, 0:2 * kp], in_=mf[:, 0:2 * kp], scalar=0.5, op=ALU.mult), inc=tag)
    e10 = p.emit("vector", lambda v: v.scalar_tensor_tensor(
        out=val[:, 0:2 * kp], in0=t1[:, 0:2 * kp], scalar=0.5, in1=val[:, 0:2 * kp],
        op0=ALU.mult, op1=ALU.add), inc=tag, waits=[(tag, max(e6, e9))])
    e11 = p.emit("vector", lambda v: v.scalar_tensor_tensor(
        out=val[:, 0:2 * kp], in0=t2[:, 0:2 * kp], scalar=0.5, in1=val[:, 0:2 * kp],
        op0=ALU.mult, op1=ALU.add), inc=tag, waits=[(tag, max(e7, e10))])
    e12 = p.emit("vector", lambda v: v.scalar_tensor_tensor(
        out=val[:, 0:2 * kp], in0=t3[:, 0:2 * kp], scalar=1.5, in1=val[:, 0:2 * kp],
        op0=ALU.mult, op1=ALU.add), inc=tag, waits=[(tag, max(e8, e11))])
    e13 = p.emit("vector", lambda v: v.tensor_copy(out=sgn[:, 0:2 * kp], in_=si[:, 0:2 * kp]),
                 inc=tag, waits=[(tag, e4)])
    e14 = p.emit("vector", lambda v: v.tensor_scalar(
        out=sgn[:, 0:2 * kp], in0=sgn[:, 0:2 * kp], scalar1=-2.0, scalar2=1.0,
        op0=ALU.mult, op1=ALU.add), inc=tag, waits=[(tag, e13)])
    e15 = p.emit("vector", lambda v: v.tensor_mul(
        out=val[:, 0:2 * kp], in0=val[:, 0:2 * kp], in1=sgn[:, 0:2 * kp]), inc=tag,
        waits=[(tag, max(e12, e14))])
    nb = kp // 16
    sc_b = sc.unsqueeze(2).broadcast_to([128, nb, 16])
    last = None
    for half in range(2):
        last = p.emit("vector", lambda v, half=half: v.tensor_tensor(
            out=out_bf[:, half::2].rearrange("p (b k) -> p b k", b=nb),
            in0=val[:, half * kp:(half + 1) * kp].rearrange("p (b k) -> p b k", b=nb),
            in1=sc_b, op=ALU.mult), inc=tag,
            waits=[(tag, e15)] + (list(out_war_waits) if half == 0 else []))
    return last


def emit_l0(p, g, io, debug=()):
    nc = p.nc
    x, wq, wk, wv, bq, bk, bv, wo, bo = (io[k] for k in
        ("x", "wq", "wk", "wv", "bq", "bk", "bv", "wo", "bo"))
    gw, gs, uw, us, dw, ds = (io[k] for k in ("gw", "gs", "uw", "us", "dw", "ds"))
    x1d = io["x1_dram"]
    ident_t = io["ident_t"]

    xnT_c = nc.dram_tensor("xnT_c", [D, RPC], BF16, kind="Internal")
    xnT_a = nc.dram_tensor("xnT_a", [NCORES * D, RPC], BF16, kind="Internal")
    wgT_c = nc.dram_tensor("wgT_c", [D, RPC], BF16, kind="Internal")
    wgT_a = nc.dram_tensor("wgT_a", [NCORES * D, RPC], BF16, kind="Internal")
    wuT_c = nc.dram_tensor("wuT_c", [D, RPC], BF16, kind="Internal")
    wuT_a = nc.dram_tensor("wuT_a", [NCORES * D, RPC], BF16, kind="Internal")
    wdT_c = nc.dram_tensor("wdT_c", [MLP0, 256], BF16, kind="Internal")
    wdT_a = nc.dram_tensor("wdT_a", [NCORES * MLP0, 256], BF16, kind="Internal")
    oT_c = nc.dram_tensor("oT_c", [NCORES * 4096, 128], BF16, kind="Internal")
    oT_x = nc.dram_tensor("oT_x", [NCORES * 4096, 128], BF16, kind="Internal")
    g0_dram = nc.dram_tensor("g0_dram", [RPC, D], BF16, kind="Internal")
    handles = {"xnT_a": xnT_a, "wgT_a": wgT_a, "wuT_a": wuT_a, "wdT_a": wdT_a,
               "oT_c": oT_c, "oT_x": oT_x, "g0_dram": g0_dram, "xnT_c": xnT_c}

    ident = g.enter_context(nc.sbuf_tensor("z_ident", [128, 128], BF16))
    ones512 = g.enter_context(nc.sbuf_tensor("z_ones512", [1, 512], BF16))
    onescol = g.enter_context(nc.sbuf_tensor("z_onescol", [128, 1], BF16))
    one1f = g.enter_context(nc.sbuf_tensor("z_one1f", [1, 1], F32))
    onescf = g.enter_context(nc.sbuf_tensor("z_onescf", [128, 1], F32))
    expc = g.enter_context(nc.sbuf_tensor("z_expc", [128, 2], F32))

    with ExitStack() as ph:
        xnT_sb = ph.enter_context(nc.sbuf_tensor("zA_xnT", [128, NCH * RPC], BF16))
        xt2 = ph.enter_context(nc.sbuf_tensor("zA_xt", [128, 2 * D], F32))
        xnbf = ph.enter_context(nc.sbuf_tensor("zA_xnbf", [128, 2 * D], BF16))
        tmp = ph.enter_context(nc.sbuf_tensor("zA_tmp", [128, D], F32))
        stat = ph.enter_context(nc.sbuf_tensor("zA_stat", [128, 8], F32))
        pst = ph.enter_context(nc.psum_tensor("zA_pst", [128, 512], BF16))

        xnT_sb_v = xnT_sb[:, :].rearrange("p (c r) -> p c r", c=NCH)

        p.dma("sync", ident[:, :], ident_t.ap(), "zld0")
        p.emit("vector", lambda v: v.memset(ones512[:, :], 1.0), inc="zc")
        p.emit("vector", lambda v: v.memset(onescol[:, :], 1.0), inc="zc")
        p.emit("vector", lambda v: v.memset(one1f[:, :], 1.0), inc="zc")
        p.emit("vector", lambda v: v.memset(onescf[:, :], 1.0), inc="zc")
        p.emit("vector", lambda v: v.memset(expc[:, 0:1], INVSQRTD), inc="zc")
        p.emit("vector", lambda v: v.memset(expc[:, 1:2], EXPSHIFT), inc="zc")

        ln_rdy = []
        mm_cnt = []
        for t in range(NT):
            xb = xt2[:, (t % 2) * D:(t % 2) * D + D]
            nb = xnbf[:, (t % 2) * D:(t % 2) * D + D]
            waits = [("zln", ln_rdy[t - 2])] if t >= 2 else []
            lv = p.dma("sync", xb, x[t * 128:(t + 1) * 128, :], "zldx",
                       waits=waits, slot=t % 2)
            p.wait_only("vector", [lv])
            if t >= 2:
                p.wait_only("vector", [("zmmA", mm_cnt[t - 2])])
            rdy = emit_ln_tile(p, xb, nb, tmp[:, :], stat, "zln")
            ln_rdy.append(rdy)
            groups = [([nb[:, ci * 128:(ci + 1) * 128] for ci in range(c0, c0 + 4)],
                       xnT_sb_v[:, c0:c0 + 4, t * 128:(t + 1) * 128])
                      for c0 in range(0, NCH, 4)]
            mv, ev = emit_transpose_groups(p, groups, ident[:, :], pst[:, :],
                                           "zmmA", "zevA",
                                           first_mm_waits=[("zln", rdy), ("zld0", 16)])
            mm_cnt.append(mv)
        p.wait_only("gpsimd", [("zevA", p.val("zevA"))])
        for ci in range(NCH):
            p.dma("gpsimd", xnT_c.ap()[ci * 128:(ci + 1) * 128, :],
                  xnT_sb[:, ci * RPC:(ci + 1) * RPC], "zxc")
        p.emit("gpsimd", lambda gp: gp.collective_compute(
            "AllGather", ALU.bypass, replica_groups=RGALL,
            ins=[xnT_c.ap().opt()], outs=[xnT_a.ap().opt()]),
            inc="zccx", waits=[("zxc", p.val("zxc"))])
        p.run_phase()

    with ExitStack() as ph:
        KPM = 1024
        stg = ph.enter_context(nc.sbuf_tensor("zB_stg", [128, NCH * RPC], BF16))
        wdq = ph.enter_context(nc.sbuf_tensor("zB_wdq", [128, 2 * 2048], BF16))
        pld = ph.enter_context(nc.sbuf_tensor("zB_pld", [128, 2 * KPM], I32))
        sld = ph.enter_context(nc.sbuf_tensor("zB_sld", [128, 2 * 64], F32))
        bufs = {k: ph.enter_context(nc.sbuf_tensor("zB_" + k, [128, 2 * KPM],
                I32 if k in ("mi", "si") else F32))
                for k in ("mi", "si", "mf", "val", "t1", "t2", "t3", "sgn")}
        dstg = ph.enter_context(nc.sbuf_tensor("zB_dstg", [128, 2 * 512], BF16))
        pst = ph.enter_context(nc.psum_tensor("zB_pst", [128, 512], BF16))

        stg_v = stg[:, :].rearrange("p (c r) -> p c r", c=NCH)

        def dq_matrix(wsrc, ssrc, contrib, agout):
            tr_cnt = []
            for nt in range(NT):
                pb = pld[:, (nt % 2) * KPM:(nt % 2) * KPM + KPM]
                sb = sld[:, (nt % 2) * 64:(nt % 2) * 64 + 64]
                lv = p.dma("sync", pb, wsrc[nt * 128:(nt + 1) * 128, :], "zdl",
                           waits=[("zdq", p.val("zdq"))], slot=nt % 2)
                lv2 = p.dma("sync", sb, ssrc[nt * 128:(nt + 1) * 128, :], "zdl",
                            slot=nt % 2)
                ob = wdq[:, (nt % 2) * 2048:(nt % 2) * 2048 + 2048]
                p.wait_only("vector", [lv2])
                war = [("zmmB", tr_cnt[nt - 2])] if len(tr_cnt) >= 2 else \
                    [("zmmB", p.val("zmmB"))]
                dv = emit_dequant_tile(p, pb, sb, ob, bufs, KPM, "zdq",
                                       out_war_waits=war)
                groups = [([ob[:, ci * 128:(ci + 1) * 128]
                            for ci in range(c0, c0 + 4)],
                           stg_v[:, c0:c0 + 4, nt * 128:(nt + 1) * 128])
                          for c0 in range(0, NCH, 4)]
                mv, ev = emit_transpose_groups(p, groups, ident[:, :], pst[:, :],
                                               "zmmB", "zevB",
                                               first_mm_waits=[("zdq", dv)],
                                               first_evac_waits=[("zwc", p.val("zwc"))])
                tr_cnt.append(mv)
            p.wait_only("gpsimd", [("zevB", p.val("zevB"))])
            for ci in range(NCH):
                p.dma("gpsimd", contrib.ap()[ci * 128:(ci + 1) * 128, :],
                      stg[:, ci * RPC:(ci + 1) * RPC], "zwc")
            p.emit("gpsimd", lambda gp, contrib=contrib, agout=agout: gp.collective_compute(
                "AllGather", ALU.bypass, replica_groups=RGALL,
                ins=[contrib.ap().opt()], outs=[agout.ap().opt()]),
                inc="zccw", waits=[("zwc", p.val("zwc"))])

        dq_matrix(gw, gs, wgT_c, wgT_a)
        dq_matrix(uw, us, wuT_c, wuT_a)
        zdc_hist = [None, None]

        for tt in range(2):
            for cc in range(8):
                idx = tt * 8 + cc
                pb = pld[:, (idx % 2) * KPM:(idx % 2) * KPM + KPM]
                sb = sld[:, (idx % 2) * 64:(idx % 2) * 64 + 64]
                lv = p.dma("sync", pb, dw[tt * 128:(tt + 1) * 128, cc * 1024:cc * 1024 + 1024],
                           "zdl", waits=[("zdq", p.val("zdq"))], slot=idx % 2)
                lv2 = p.dma("sync", sb, ds[tt * 128:(tt + 1) * 128, cc * 64:cc * 64 + 64],
                            "zdl", slot=idx % 2)
                ob = wdq[:, (idx % 2) * 2048:(idx % 2) * 2048 + 2048]
                p.wait_only("vector", [lv2])
                dv = emit_dequant_tile(p, pb, sb, ob, bufs, KPM, "zdq",
                                       out_war_waits=[("zmmB", p.val("zmmB"))])
                from concourse.ap import AP as _AP
                for c0 in range(0, NCH, 4):
                    sl_i = ((idx * 4) + c0 // 4) % 2
                    slot = dstg[:, sl_i * 512: sl_i * 512 + 512]
                    _, ev = emit_transpose_groups(
                        p, [([ob[:, ci * 128:(ci + 1) * 128]
                              for ci in range(c0, c0 + 4)], slot)],
                        ident[:, :], pst[:, :], "zmmB", "zevB",
                        first_mm_waits=[("zdq", dv)],
                        first_evac_waits=(
                            [zdc_hist[sl_i]] if zdc_hist[sl_i] else []),
                        evac="vector")
                    dst = _AP(wdT_c.ap().tensor,
                              (cc * 2048 + c0 * 128) * 256 + tt * 128,
                              [[256, 128], [128 * 256, 4], [1, 128]])
                    zdc_hist[sl_i] = p.dma("gpsimd", dst,
                          slot.rearrange("p (c j) -> p c j", c=4),
                          "zdc", waits=[("zevB", ev)], slot=sl_i)
        p.emit("gpsimd", lambda gp: gp.collective_compute(
            "AllGather", ALU.bypass, replica_groups=RGALL,
            ins=[wdT_c.ap().opt()], outs=[wdT_a.ap().opt()]),
            inc="zccw", waits=p.dma_totals("zdc"))
        p.run_phase()

    with ExitStack() as ph:
        wq_sb = ph.enter_context(nc.sbuf_tensor("zC_wq", [128, NCH * 256], BF16))
        wk_sb = ph.enter_context(nc.sbuf_tensor("zC_wk", [128, NCH * 256], BF16))
        wv_sb = ph.enter_context(nc.sbuf_tensor("zC_wv", [128, NCH * 256], BF16))
        bq_sb = ph.enter_context(nc.sbuf_tensor("zC_bq", [1, 256], BF16))
        bk_sb = ph.enter_context(nc.sbuf_tensor("zC_bk", [1, 256], BF16))
        bv_sb = ph.enter_context(nc.sbuf_tensor("zC_bv", [1, 256], BF16))
        qt_sb = [ph.enter_context(nc.sbuf_tensor("zC_qt%d" % j, [128, S], BF16))
                 for j in range(2)]
        kt_sb = [ph.enter_context(nc.sbuf_tensor("zC_kt%d" % j, [128, S], BF16))
                 for j in range(2)]
        v_sb = [ph.enter_context(nc.sbuf_tensor("zC_v%d" % j, [128, S], BF16))
                for j in range(2)]
        xnt_ld = ph.enter_context(nc.sbuf_tensor("zC_xnt", [128, NCH * 512], BF16))
        ot_sb = ph.enter_context(nc.sbuf_tensor("zC_ot", [128, S], BF16))
        pt_sb = ph.enter_context(nc.sbuf_tensor("zC_pt", [128, 2048], BF16))
        lacc = ph.enter_context(nc.sbuf_tensor("zC_lacc", [128, 2 * 512], F32))
        ltmp = ph.enter_context(nc.sbuf_tensor("zC_ltmp", [128, 2 * 512], F32))
        lbuf = ph.enter_context(nc.sbuf_tensor("zC_lbuf", [1, 2 * 512], F32))
        linv = ph.enter_context(nc.sbuf_tensor("zC_linv", [128, 64], F32))
        olhsT = ph.enter_context(nc.sbuf_tensor("zC_olhsT", [128, S], BF16))

        for wsb, wsrc in ((wq_sb, wq), (wk_sb, wk), (wv_sb, wv)):
            p.dma("gpsimd", wsb[:, :].rearrange("p (c n) -> p c n", c=NCH),
                  wsrc.rearrange("(c p) n -> p c n", p=128), "zqw")
        for bsb, bsrc in ((bq_sb, bq), (bk_sb, bk), (bv_sb, bv)):
            p.dma("gpsimd", bsb[:, :], bsrc, "zqw")
        p.wait_only("sync", [("zccx", 1), ("zqw", p.val("zqw"))])

        for b in range(2):
            with ExitStack() as ps:
                psq = [ps.enter_context(nc.psum_tensor("zC_psq%d%d" % (b, j), [128, 512], F32))
                       for j in range(2)]
                psk = [ps.enter_context(nc.psum_tensor("zC_psk%d%d" % (b, j), [128, 512], F32))
                       for j in range(2)]
                psv = [ps.enter_context(nc.psum_tensor("zC_psv%d%d" % (b, s_), [128, 256], F32))
                       for s_ in range(4)]
                for rg in range(16):
                    i = rg // 2
                    xball = xnt_ld
                    lv = None
                    for chalf in range(2):
                        w = [("zqmm", p.val("zqmm"))]
                        lv = p.dma("sync",
                                   xball[:, chalf * 4096:(chalf + 1) * 4096]
                                   .rearrange("p (c n) -> p c n", c=8),
                                   xnT_a.ap()[i * D + chalf * 1024:
                                              i * D + (chalf + 1) * 1024,
                                              b * PPC + (rg % 2) * 512:
                                              b * PPC + (rg % 2) * 512 + 512]
                                   .rearrange("(c p) n -> p c n", p=128),
                                   "zxl", waits=w, slot=chalf)
                    for ci in range(NCH):
                        xb = xball[:, ci * 512:ci * 512 + 512]
                        for j in range(2):
                            p.emit("tensor", lambda t, j=j, ci=ci, xb=xb: t.matmul(
                                out=psq[j][:, :],
                                lhsT=wq_sb[:, ci * 256 + j * 128: ci * 256 + j * 128 + 128],
                                rhs=xb, start=(ci == 0), stop=False), inc="zqmm",
                                waits=([lv] if j == 0 and ci == 0 else []) +
                                ([("zqev", p.val("zqev"))] if ci == 0 and j == 0 else []))
                            p.emit("tensor", lambda t, j=j, ci=ci, xb=xb: t.matmul(
                                out=psk[j][:, :],
                                lhsT=wk_sb[:, ci * 256 + j * 128: ci * 256 + j * 128 + 128],
                                rhs=xb, start=(ci == 0), stop=False), inc="zqmm")
                        for s_ in range(4):
                            p.emit("tensor", lambda t, s_=s_, ci=ci, xb=xb: t.matmul(
                                out=psv[s_][:, :], lhsT=xb[:, s_ * 128:s_ * 128 + 128],
                                rhs=wv_sb[:, ci * 256: ci * 256 + 256],
                                start=(ci == 0), stop=False), inc="zqmm")
                    for j in range(2):
                        bq_mm = p.emit("tensor", lambda t, j=j: t.matmul(
                            out=psq[j][:, :], lhsT=bq_sb[:, j * 128:j * 128 + 128],
                            rhs=ones512[:, :], start=False, stop=True), inc="zqmm")
                        bk_mm = p.emit("tensor", lambda t, j=j: t.matmul(
                            out=psk[j][:, :], lhsT=bk_sb[:, j * 128:j * 128 + 128],
                            rhs=ones512[:, :], start=False, stop=True), inc="zqmm")
                        p.emit("vector", lambda v, j=j, rg=rg: v.tensor_copy(
                            out=qt_sb[j][:, rg * 512:(rg + 1) * 512], in_=psq[j][:, :]),
                            inc="zqev", waits=[("zqmm", bq_mm)])
                        p.emit("vector", lambda v, j=j, rg=rg: v.tensor_copy(
                            out=kt_sb[j][:, rg * 512:(rg + 1) * 512], in_=psk[j][:, :]),
                            inc="zqev", waits=[("zqmm", bk_mm)])
                    for s_ in range(4):
                        bv_mm = p.emit("tensor", lambda t, s_=s_: t.matmul(
                            out=psv[s_][:, :], lhsT=ones512[:, 0:128],
                            rhs=bv_sb[:, :], start=False, stop=True), inc="zqmm")
                        for j in range(2):
                            st = rg * 4 + s_
                            p.emit("vector", lambda v, s_=s_, j=j, st=st: v.tensor_copy(
                                out=v_sb[j][:, st * 128:(st + 1) * 128],
                                in_=psv[s_][:, j * 128:j * 128 + 128]),
                                inc="zqev", waits=[("zqmm", bv_mm)])
            qkv_done = p.val("zqev")
            for j in range(2):
                with ExitStack() as ps:
                    s_ps2 = [ps.enter_context(nc.psum_tensor("zC_S%d%d%d" % (b, j, z), [128, 1024], F32))
                             for z in range(2)]
                    o_ps = ps.enter_context(nc.psum_tensor("zC_O%d%d" % (b, j), [128, 512], F32))
                    l_ps = ps.enter_context(nc.psum_tensor("zC_L%d%d" % (b, j), [1, 512], F32))
                    lt_ps = ps.enter_context(nc.psum_tensor("zC_LT%d%d" % (b, j), [128, 64], F32))
                    pstb = ps.enter_context(nc.psum_tensor("zC_pst%d%d" % (b, j), [128, 128], BF16))
                    av_hist = [0, 0]
                    ex_hist = [0, 0]
                    lv_hist = [0, 0]
                    ex_of = {}
                    for qg in range(16):
                        for kg in range(33):
                            par = kg % 2
                            if kg < 32:
                                s_ps = s_ps2[par]
                                ptb = pt_sb[:, par * 1024:par * 1024 + 1024]
                                smm = None
                                for jj in range(2):
                                    kc = kg * 2 + jj
                                    w = []
                                    if jj == 0:
                                        w = [("zexp", ex_hist[par])]
                                        if qg == 0 and kg == 0:
                                            w.append(("zqev", qkv_done))
                                    smm = p.emit("tensor", lambda t, jj=jj, kc=kc, qg=qg, j=j, s_ps=s_ps: t.matmul(
                                        out=s_ps[:, jj * 512:jj * 512 + 512],
                                        lhsT=kt_sb[j][:, kc * 128:(kc + 1) * 128],
                                        rhs=qt_sb[j][:, qg * 512:(qg + 1) * 512],
                                        start=True, stop=True), inc="zsmm", waits=w)
                                ex = p.emit("scalar", lambda sc, s_ps=s_ps, ptb=ptb: sc.activation(
                                    out=ptb, in_=s_ps[:, :], func=AF.Exp,
                                    bias=expc[:, 1:2], scale=expc[:, 0:1]), inc="zexp",
                                    waits=[("zsmm", smm), ("zav", av_hist[par]),
                                           ("zlv", lv_hist[par])])
                                ex_hist[par] = ex
                                ex_of[(qg, kg)] = ex
                            if kg >= 1:
                                kgp = kg - 1
                                parp = kgp % 2
                                ptb = pt_sb[:, parp * 1024:parp * 1024 + 1024]
                                for jj in range(2):
                                    kc = kgp * 2 + jj
                                    first = (kgp == 0 and jj == 0)
                                    lastm = (kgp == 31 and jj == 1)
                                    w = [("zexp", ex_of[(qg, kgp)])] if jj == 0 else []
                                    if first:
                                        w.append(("zoev", p.val("zoev")))
                                    av_hist[parp] = p.emit("tensor", lambda t, jj=jj, kc=kc, j=j, first=first, lastm=lastm, ptb=ptb: t.matmul(
                                        out=o_ps[:, :], lhsT=v_sb[j][:, kc * 128:(kc + 1) * 128],
                                        rhs=ptb[:, jj * 512:jj * 512 + 512],
                                        start=first, stop=lastm), inc="zav", waits=w)
                                la = lacc[:, parp * 512:parp * 512 + 512]
                                lt = ltmp[:, parp * 512:parp * 512 + 512]
                                t1 = p.emit("vector", lambda v, ptb=ptb, lt=lt: v.tensor_add(
                                    out=lt, in0=ptb[:, 0:512], in1=ptb[:, 512:1024]),
                                    inc="zlv", waits=[("zexp", ex_of[(qg, kgp)]),
                                                      ("zlv", lv_hist[parp])])
                                if kgp < 2:
                                    lv_hist[parp] = p.emit("vector", lambda v, la=la, lt=lt: v.tensor_copy(
                                        out=la, in_=lt), inc="zlv", waits=[("zlv", t1)])
                                else:
                                    lv_hist[parp] = p.emit("vector", lambda v, la=la, lt=lt: v.tensor_add(
                                        out=la, in0=la, in1=lt), inc="zlv",
                                        waits=[("zlv", t1)])
                        for z in range(2):
                            lmm = p.emit("tensor", lambda t, z=z: t.matmul(
                                out=l_ps[:, :], lhsT=onescf[:, :],
                                rhs=lacc[:, z * 512:(z + 1) * 512],
                                start=(z == 0), stop=(z == 1)), inc="zav",
                                waits=[("zlv", lv_hist[z])] + (
                                    [("zoev", p.val("zoev"))] if z == 0 else []))
                        ov = p.emit("vector", lambda v, qg=qg: v.tensor_copy(
                            out=ot_sb[:, qg * 512:(qg + 1) * 512], in_=o_ps[:, :]),
                            inc="zoev", waits=[("zav", p.val("zav"))])
                        lb = lbuf[:, (qg % 2) * 512:(qg % 2) * 512 + 512]
                        lev = p.emit("vector", lambda v, lb=lb: v.tensor_copy(
                            out=lb, in_=l_ps[0:1, :]),
                            inc="zoev", waits=[("zoev", ov), ("zlt", p.val("zlt"))])
                        lt_last = None
                        for z in range(4):
                            w = [("zoev", lev)] if z == 0 else []
                            if qg == 0 and z == 0:
                                w.append(("znrm", p.val("znrm")))
                            lt_last = p.emit("tensor", lambda t, z=z, qg=qg, lb=lb: t.matmul(
                                out=lt_ps[:, qg * 4 + z: qg * 4 + z + 1],
                                lhsT=lb[:, z * 128:(z + 1) * 128],
                                rhs=one1f[:, :], start=True, stop=True), inc="zlt",
                                waits=w)
                    rcp = p.emit("vector", lambda v: v.reciprocal(
                        out=linv[:, :], in_=lt_ps[:, :]), inc="zoev",
                        waits=[("zlt", p.val("zlt"))])
                    view4 = olhsT[:, :].rearrange("p (t dd s) -> p t dd s",
                                                  t=64, dd=2, s=64)
                    nlast = None
                    for sb_ in range(64):
                        w = [("znrm", p.val("znrm"))]
                        if sb_ == 0:
                            w += [("zoev", rcp), ("zolc", p.val("zolc"))]
                        mv_ = p.emit("tensor", lambda t, sb_=sb_: t.transpose(
                            out=pstb[:, :], in_=ot_sb[:, sb_ * 128:(sb_ + 1) * 128],
                            identity=ident[:, :]), inc="zscr", waits=w)
                        nlast = p.emit("vector", lambda v, sb_=sb_, view4=view4: v.tensor_scalar(
                            out=view4[:, :, :, sb_],
                            in0=pstb[:, :].rearrange("p (t dd) -> p t dd", dd=2),
                            scalar1=linv[:, sb_:sb_ + 1], scalar2=None,
                            op0=ALU.mult), inc="znrm", waits=[("zscr", mv_)])
                    pidx = b * 2 + j
                    from concourse.ap import AP as _AP
                    for ts in range(NCORES):
                        dstv = _AP(oT_c.ap().tensor,
                                   (ts * 4096 + pidx * 1024) * 128,
                                   [[128, 128], [128 * 128, 8], [1, 128]])
                        p.dma("gpsimd", dstv,
                              olhsT[:, ts * 1024:(ts + 1) * 1024]
                              .rearrange("p (tl j) -> p tl j", tl=8),
                              "zolc", waits=[("znrm", nlast)] if ts == 0 else [])
        from concourse.aot_env import allow_ici_all_to_all

        def _a2a(gp):
            with allow_ici_all_to_all():
                return gp.collective_compute(
                    "AllToAll", ALU.bypass, replica_groups=RGALL,
                    ins=[oT_c.ap().opt()], outs=[oT_x.ap().opt()])
        p.emit("gpsimd", _a2a, inc="zcco", waits=[("zolc", p.val("zolc"))])
        p.run_phase()

    with ExitStack() as ph:
        xnT_rg = ph.enter_context(nc.sbuf_tensor("zD_xnrg", [128, NCH * 1024], BF16))
        interT = ph.enter_context(nc.sbuf_tensor("zD_inter", [128, 32 * 1024], BF16))
        gaccT = ph.enter_context(nc.sbuf_tensor("zD_gacc", [128, NCH * 1024], F32))
        wgt_ld = ph.enter_context(nc.sbuf_tensor("zD_wg", [128, 2 * 2048], BF16))
        wut_ld = ph.enter_context(nc.sbuf_tensor("zD_wu", [128, 2 * 2048], BF16))
        wdt_ld = ph.enter_context(nc.sbuf_tensor("zD_wd", [128, 2 * 4096], BF16))
        pgb = ph.enter_context(nc.sbuf_tensor("zD_pgb", [128, 1024], BF16))
        pub = ph.enter_context(nc.sbuf_tensor("zD_pub", [128, 1024], BF16))
        gqb = ph.enter_context(nc.sbuf_tensor("zD_gqb", [128, 1024], BF16))
        tib = ph.enter_context(nc.sbuf_tensor("zD_tib", [128, 1024], BF16))
        g0s = ph.enter_context(nc.sbuf_tensor("zD_g0s", [128, 1024], BF16))
        g0stg = ph.enter_context(nc.sbuf_tensor("zD_g0stg", [128, 2 * 512], BF16))
        pst = ph.enter_context(nc.psum_tensor("zD_pst", [128, 512], BF16))
        psgs = [ph.enter_context(nc.psum_tensor("zD_psg%d" % i, [128, 512], F32))
                for i in range(2)]
        psus = [ph.enter_context(nc.psum_tensor("zD_psu%d" % i, [128, 512], F32))
                for i in range(2)]
        psd = ph.enter_context(nc.psum_tensor("zD_psd", [128, 1024], F32))

        p.wait_only("sync", [("zccw", 3)])
        zg0_hist = [None, None]
        evg = [0, 0]
        evu = [0, 0]
        chain_done = [0, 0]
        for RG in range(2):
            r0 = RG * 1024
            for ci in range(NCH):
                p.dma("sync", xnT_rg[:, ci * 1024:(ci + 1) * 1024],
                      xnT_c.ap()[ci * 128:(ci + 1) * 128, r0:r0 + 1024], "zfx",
                      waits=[("zgmm", p.val("zgmm"))] if ci == 0 else [])
            xn_ld = p.val("zfx")
            for q4 in range(4):
                for ns in range(32):
                    n0 = q4 * 4096 + ns * 128
                    i = n0 // 2048
                    nl = n0 % 2048
                    gb = wgt_ld[:, (ns % 2) * 2048:(ns % 2) * 2048 + 2048]
                    ub = wut_ld[:, (ns % 2) * 2048:(ns % 2) * 2048 + 2048]
                    w = [("zgmm", max(0, p.val("zgmm") - 64))]
                    lg_ = p.dma("sync", gb.rearrange("p (c n) -> p c n", c=NCH),
                                wgT_a.ap()[i * D:(i + 1) * D, nl:nl + 128]
                                .rearrange("(c p) n -> p c n", p=128), "zfwg",
                                waits=w, slot=ns % 2)
                    lu_ = p.dma("sync", ub.rearrange("p (c n) -> p c n", c=NCH),
                                wuT_a.ap()[i * D:(i + 1) * D, nl:nl + 128]
                                .rearrange("(c p) n -> p c n", p=128), "zfwu",
                                slot=ns % 2)
                    gmm = umm = None
                    for ci in range(NCH):
                        for rh in range(2):
                            w = []
                            if ci == 0:
                                w = [("zgl", evg[rh])]
                                if rh == 0:
                                    w += [lg_, ("zfx", xn_ld)]
                            gmm = p.emit("tensor", lambda t, ci=ci, rh=rh, gb=gb: t.matmul(
                                out=psgs[rh][:, :],
                                lhsT=gb[:, ci * 128:(ci + 1) * 128],
                                rhs=xnT_rg[:, ci * 1024 + rh * 512: ci * 1024 + rh * 512 + 512],
                                start=(ci == 0), stop=(ci == NCH - 1)), inc="zgmm",
                                waits=w)
                            w = []
                            if ci == 0:
                                w = [("zgl", evu[rh])]
                                if rh == 0:
                                    w += [lu_]
                            umm = p.emit("tensor", lambda t, ci=ci, rh=rh, ub=ub: t.matmul(
                                out=psus[rh][:, :],
                                lhsT=ub[:, ci * 128:(ci + 1) * 128],
                                rhs=xnT_rg[:, ci * 1024 + rh * 512: ci * 1024 + rh * 512 + 512],
                                start=(ci == 0), stop=(ci == NCH - 1)), inc="zgmm",
                                waits=w)
                    for rh in range(2):
                        pg = psgs[rh][:, :]
                        pu = psus[rh][:, :]
                        pgs = pgb[:, rh * 512:rh * 512 + 512]
                        pus = pub[:, rh * 512:rh * 512 + 512]
                        gq = gqb[:, rh * 512:rh * 512 + 512]
                        ti = tib[:, rh * 512:rh * 512 + 512]
                        c0 = p.emit("scalar", lambda sc, pg=pg, pgs=pgs: sc.copy(
                            out=pgs, in_=pg), inc="zgl",
                            waits=[("zgmm", umm if rh == 1 else gmm),
                                   ("zgl", chain_done[rh])])
                        evg[rh] = c0
                        c1 = p.emit("scalar", lambda sc, pu=pu, pus=pus: sc.copy(
                            out=pus, in_=pu), inc="zgl", waits=[("zgl", c0)])
                        evu[rh] = c1
                        a1 = p.emit("scalar", lambda sc, pgs=pgs, gq=gq: sc.activation(
                            out=gq, in_=pgs, func=AF.Square), inc="zgl",
                            waits=[("zgl", c1)])
                        a2 = p.emit("vector", lambda v, gq=gq: v.tensor_scalar(
                            out=gq, in0=gq,
                            scalar1=0.044715 * 0.7978845608028654,
                            scalar2=0.7978845608028654,
                            op0=ALU.mult, op1=ALU.add), inc="zgl", waits=[("zgl", a1)])
                        a3 = p.emit("vector", lambda v, pgs=pgs, gq=gq, ti=ti: v.scalar_tensor_tensor(
                            out=ti, in0=gq, scalar=1.0, in1=pgs,
                            op0=ALU.mult, op1=ALU.mult), inc="zgl", waits=[("zgl", a2)])
                        a4 = p.emit("scalar", lambda sc, ti=ti, gq=gq: sc.activation(
                            out=gq, in_=ti, func=AF.Tanh),
                            inc="zgl", waits=[("zgl", a3)])
                        a5 = p.emit("vector", lambda v, gq=gq: v.tensor_scalar(
                            out=gq, in0=gq, scalar1=0.5,
                            scalar2=0.5, op0=ALU.mult, op1=ALU.add), inc="zgl",
                            waits=[("zgl", a4)])
                        a6 = p.emit("vector", lambda v, pgs=pgs, ti=ti, gq=gq: v.tensor_tensor(
                            out=ti, in0=gq, in1=pgs,
                            op=ALU.mult), inc="zgl", waits=[("zgl", a5)])
                        a7 = p.emit("vector", lambda v, pus=pus, ns=ns, rh=rh, ti=ti: v.tensor_tensor(
                            out=interT[:, ns * 1024 + rh * 512: ns * 1024 + rh * 512 + 512],
                            in0=ti, in1=pus, op=ALU.mult), inc="zgl",
                            waits=[("zgl", a6)])
                        chain_done[rh] = a7
                inter_done = p.val("zgl")
                for ocs in range(NCH):
                    jsh = ocs // 2
                    oc0 = (ocs % 2) * 128
                    db = wdt_ld[:, (ocs % 2) * 4096:(ocs % 2) * 4096 + 4096]
                    w = [("zgmm", max(0, p.val("zgmm") - 64))]
                    ld_ = p.dma("sync", db.rearrange("p (c n) -> p c n", c=32),
                                wdT_a.ap()[jsh * MLP0 + q4 * 4096: jsh * MLP0 + (q4 + 1) * 4096,
                                           oc0:oc0 + 128]
                                .rearrange("(c p) n -> p c n", p=128), "zfwd",
                                waits=w, slot=ocs % 2)
                    dmm = None
                    for ch in range(32):
                        for rh in range(2):
                            dmm = p.emit("tensor", lambda t, ch=ch, rh=rh, db=db: t.matmul(
                                out=psd[:, rh * 512:rh * 512 + 512],
                                lhsT=db[:, ch * 128:(ch + 1) * 128],
                                rhs=interT[:, ch * 1024 + rh * 512: ch * 1024 + rh * 512 + 512],
                                start=(ch == 0), stop=(ch == 31)), inc="zgmm",
                                waits=([ld_, ("zgl", inter_done),
                                        ("zacc", p.val("zacc"))] if ch == 0 and rh == 0 else []))
                    if q4 == 0:
                        p.emit("vector", lambda v, ocs=ocs: v.tensor_copy(
                            out=gaccT[:, ocs * 1024:(ocs + 1) * 1024], in_=psd[:, :]),
                            inc="zacc", waits=[("zgmm", dmm)])
                    else:
                        p.emit("vector", lambda v, ocs=ocs: v.tensor_add(
                            out=gaccT[:, ocs * 1024:(ocs + 1) * 1024],
                            in0=gaccT[:, ocs * 1024:(ocs + 1) * 1024], in1=psd[:, :]),
                            inc="zacc", waits=[("zgmm", dmm)])
            for ocs in range(NCH):
                gsb = g0s[:, 0:1024]
                s3 = None
                for hh in range(2):
                    ga = gaccT[:, ocs * 1024 + hh * 512: ocs * 1024 + hh * 512 + 512]
                    sil = gqb[:, hh * 512:hh * 512 + 512]
                    base = p.val("zsil")
                    s1 = p.emit("scalar", lambda sc, ga=ga, sil=sil: sc.activation(
                        out=sil, in_=ga, func=AF.Tanh, scale=0.5), inc="zsil",
                        waits=[("zacc", p.val("zacc")), ("zsil", base),
                               ("zgl", p.val("zgl"))])
                    s2 = p.emit("vector", lambda v, sil=sil: v.tensor_scalar(
                        out=sil, in0=sil, scalar1=0.5, scalar2=0.5,
                        op0=ALU.mult, op1=ALU.add), inc="zsil", waits=[("zsil", s1)])
                    s3 = p.emit("vector", lambda v, ga=ga, gsb=gsb, hh=hh, sil=sil: v.tensor_mul(
                        out=gsb[:, hh * 512:hh * 512 + 512], in0=ga, in1=sil),
                        inc="zsil", waits=[("zsil", s2), ("zmmD", p.val("zmmD"))])
                from concourse.ap import AP as _AP
                for s0_ in range(0, 8, 4):
                    sl_i = (s0_ // 4) % 2
                    stgb = g0stg[:, sl_i * 512: sl_i * 512 + 512]
                    _, ev = emit_transpose_groups(
                        p, [([gsb[:, sub * 128:(sub + 1) * 128]
                              for sub in range(s0_, s0_ + 4)], stgb)],
                        ident[:, :], pst[:, :], "zmmD", "zevD",
                        first_mm_waits=[("zsil", s3)],
                        first_evac_waits=(
                            [zg0_hist[sl_i]] if zg0_hist[sl_i] else []),
                        evac="vector")
                    dst = _AP(g0_dram.ap().tensor,
                              (r0 + s0_ * 128) * D + ocs * 128,
                              [[D, 128], [128 * D, 4], [1, 128]])
                    zg0_hist[sl_i] = p.dma("gpsimd", dst,
                          stgb.rearrange("p (c j) -> p c j", c=4),
                          "zg0", waits=[("zevD", ev)], slot=sl_i)
        p.wait_only("gpsimd", p.dma_totals("zg0"))
        p.run_phase()

    with ExitStack() as ph:
        wo_sb = ph.enter_context(nc.sbuf_tensor("zE_wo", [128, HEADS * D], BF16))
        bo_sb = ph.enter_context(nc.sbuf_tensor("zE_bo", [1, D], BF16))
        t_ld = ph.enter_context(nc.sbuf_tensor("zE_t", [128, 2 * HEADS * 128], BF16))
        g0c = ph.enter_context(nc.sbuf_tensor("zE_g0", [128, 2 * D], BF16))
        xc = ph.enter_context(nc.sbuf_tensor("zE_x", [128, 2 * D], F32))
        x1st = ph.enter_context(nc.sbuf_tensor("zE_x1", [128, 2 * D], F32))
        psE = [ph.enter_context(nc.psum_tensor("zE_ps%d" % i, [128, 2048], F32))
               for i in range(2)]

        for h in range(HEADS):
            p.dma("gpsimd", wo_sb[:, h * D:(h + 1) * D],
                  wo.rearrange("(sl h) j -> h sl j", h=16)[h], "zwo")
        p.dma("gpsimd", bo_sb[:, :], bo, "zwo")
        p.wait_only("sync", [("zcco", 1)])
        p.wait_only("tensor", [("zwo", p.val("zwo"))])
        zx1_hist = [None, None]
        zfin_slot = [0, 0]
        for tau in range(NT):
            b_ = tau // 8
            tl = tau % 8
            ps = psE[tau % 2]
            tb = t_ld[:, (tau % 2) * HEADS * 128:(tau % 2) * HEADS * 128 + HEADS * 128]
            w = [("zemm", p.val("zemm") - 68)] if tau >= 2 else []
            tld = None
            for h in range(HEADS):
                i = h // 2
                pidx = b_ * 2 + (h % 2)
                row0 = i * 4096 + pidx * 1024 + tl * 128
                tld = p.dma("sync", tb[:, h * 128:(h + 1) * 128],
                            oT_x.ap()[row0:row0 + 128, :], "ztl",
                            waits=w if h == 0 else [], slot=tau % 2)
            lgc = p.dma("sync", g0c[:, (tau % 2) * D:(tau % 2) * D + D],
                        g0_dram.ap()[tau * 128:(tau + 1) * 128, :], "ztg",
                        slot=tau % 2)
            lxc = p.dma("sync", xc[:, (tau % 2) * D:(tau % 2) * D + D],
                        x[tau * 128:(tau + 1) * 128, :], "ztx", slot=tau % 2)
            mml = None
            for jg in range(4):
                for h in range(HEADS):
                    mml = p.emit("tensor", lambda t, jg=jg, h=h, ps=ps, tb=tb: t.matmul(
                        out=ps[:, jg * 512:jg * 512 + 512],
                        lhsT=tb[:, h * 128:(h + 1) * 128],
                        rhs=wo_sb[:, h * D + jg * 512: h * D + jg * 512 + 512],
                        start=(h == 0), stop=False), inc="zemm",
                        waits=([tld, ("zfin", p.val("zfin"))]
                               if jg == 0 and h == 0 else []))
                mml = p.emit("tensor", lambda t, jg=jg, ps=ps: t.matmul(
                    out=ps[:, jg * 512:jg * 512 + 512], lhsT=ones512[:, 0:128],
                    rhs=bo_sb[:, jg * 512:jg * 512 + 512],
                    start=False, stop=True), inc="zemm")
            gslice = g0c[:, (tau % 2) * D:(tau % 2) * D + D]
            xslice = xc[:, (tau % 2) * D:(tau % 2) * D + D]
            oslice = x1st[:, (tau % 2) * D:(tau % 2) * D + D]
            f1 = p.emit("vector", lambda v, ps=ps, gslice=gslice, oslice=oslice: v.tensor_tensor(
                out=oslice, in0=ps[:, :], in1=gslice, op=ALU.mult), inc="zfin",
                waits=[("zemm", mml), lgc] +
                      ([zx1_hist[tau % 2]] if zx1_hist[tau % 2] else []))
            zfin_slot[tau % 2] = f1
            f2 = p.emit("vector", lambda v, oslice=oslice, xslice=xslice: v.tensor_add(
                out=oslice, in0=oslice, in1=xslice), inc="zfin",
                waits=[("zfin", f1), lxc])
            zx1_hist[tau % 2] = p.dma("gpsimd", x1d[tau * 128:(tau + 1) * 128, :],
                                      oslice, "zx1", waits=[("zfin", f2)],
                                      slot=tau % 2)
        p.wait_only("gpsimd", p.dma_totals("zx1"))
        p.run_phase()

    return handles


def build_l0_nc(debug=()):
    nc = bass.Bass("TRN2", target_bir_lowering=False)
    p = Prog(nc)
    io = {}
    io["x"] = nc.dram_tensor("x", [RPC, D], F32, kind="ExternalInput").ap()
    io["wq"] = nc.dram_tensor("wq", [D, 256], F32, kind="ExternalInput").ap()
    io["wk"] = nc.dram_tensor("wk", [D, 256], F32, kind="ExternalInput").ap()
    io["wv"] = nc.dram_tensor("wv", [D, 256], F32, kind="ExternalInput").ap()
    io["bq"] = nc.dram_tensor("bq", [1, 256], F32, kind="ExternalInput").ap()
    io["bk"] = nc.dram_tensor("bk", [1, 256], F32, kind="ExternalInput").ap()
    io["bv"] = nc.dram_tensor("bv", [1, 256], F32, kind="ExternalInput").ap()
    io["wo"] = nc.dram_tensor("wo", [D, D], F32, kind="ExternalInput").ap()
    io["bo"] = nc.dram_tensor("bo", [1, D], F32, kind="ExternalInput").ap()
    io["gw"] = nc.dram_tensor("gw", [D, 1024], I32, kind="ExternalInput").ap()
    io["gs"] = nc.dram_tensor("gs", [D, 64], F32, kind="ExternalInput").ap()
    io["uw"] = nc.dram_tensor("uw", [D, 1024], I32, kind="ExternalInput").ap()
    io["us"] = nc.dram_tensor("us", [D, 64], F32, kind="ExternalInput").ap()
    io["dw"] = nc.dram_tensor("dw", [256, S], I32, kind="ExternalInput").ap()
    io["ds"] = nc.dram_tensor("ds", [256, 512], F32, kind="ExternalInput").ap()
    io["x1_dram"] = nc.dram_tensor("x1out", [RPC, D], F32, kind="ExternalOutput").ap()
    io["ident_t"] = nc.inline_tensor(np.eye(128, dtype=ml_dtypes.bfloat16), "ident_np")
    dbgaps = {}
    dbg_shapes = {
        "xnT_a": ([NCORES * D, RPC], BF16), "wgT_a": ([NCORES * D, RPC], BF16),
        "wuT_a": ([NCORES * D, RPC], BF16), "wdT_a": ([NCORES * MLP0, 256], BF16),
        "oT_c": ([NCORES * 4096, 128], BF16), "oT_x": ([NCORES * 4096, 128], BF16),
        "g0_dram": ([RPC, D], BF16),
    }
    for name in debug:
        if name in dbg_shapes:
            shp, dt = dbg_shapes[name]
            dbgaps[name] = nc.dram_tensor("dbg_" + name, shp, dt,
                                          kind="ExternalOutput").ap()
    with ExitStack() as g:
        handles = emit_l0(p, g, io, debug)
        if dbgaps:
            for name, ap in dbgaps.items():
                p.dma("gpsimd", ap, handles[name].ap(), "dbg")
            p.wait_only("gpsimd", [("dbg", p.val("dbg"))])
            p.run_phase()
    return nc


def build_l1_nc(debug=()):
    nc = bass.Bass("TRN2", target_bir_lowering=False)
    p = Prog(nc)
    io = {}
    io["x1"] = nc.dram_tensor("x1", [RPC, D], F32, kind="ExternalInput").ap()
    io["qkvw"] = nc.dram_tensor("qkvw", [D, 3072], F32, kind="ExternalInput").ap()
    io["qkvb"] = nc.dram_tensor("qkvb", [1, 3072], F32, kind="ExternalInput").ap()
    io["outw"] = nc.dram_tensor("outw", [D, D], F32, kind="ExternalInput").ap()
    io["outb"] = nc.dram_tensor("outb", [1, D], F32, kind="ExternalInput").ap()
    io["wfc"] = nc.dram_tensor("wfc", [H1, D], F32, kind="ExternalInput").ap()
    io["wproj"] = nc.dram_tensor("wproj", [D, H1], F32, kind="ExternalInput").ap()
    io["y"] = nc.dram_tensor("y", [RPC, D], F32, kind="ExternalOutput").ap()
    io["g1_dram"] = nc.dram_tensor("g1_dram", [RPC, D], BF16, kind="Internal")
    io["ident_t"] = nc.inline_tensor(np.eye(128, dtype=ml_dtypes.bfloat16), "ident_np")
    dbgaps = {}
    for name, shape, dt in [("mu_sb", [128, 8], F32), ("xn1T", [128, NCH * RPC], BF16),
                            ("g1", [RPC, D], BF16), ("attnT", [128, NCH * RPC], BF16)]:
        if name in debug:
            dbgaps[name] = nc.dram_tensor("dbg_" + name, shape, dt,
                                          kind="ExternalOutput").ap()
    with ExitStack() as g:
        handles = emit_l1(p, g, io, debug)
        if dbgaps:
            for name, ap in dbgaps.items():
                if name == "g1":
                    p.dma("gpsimd", ap, io["g1_dram"].ap(), "dbg")
                else:
                    p.dma("gpsimd", ap, handles[name][:, :], "dbg")
            p.wait_only("gpsimd", [("dbg", p.val("dbg"))])
            p.run_phase()
    return nc


def build_full_nc(debug=()):
    nc = bass.Bass("TRN2", target_bir_lowering=False)
    p = Prog(nc)
    ident_t = nc.inline_tensor(np.eye(128, dtype=ml_dtypes.bfloat16), "ident_np")
    io0 = {}
    io0["x"] = nc.dram_tensor("x", [RPC, D], F32, kind="ExternalInput").ap()
    io0["wq"] = nc.dram_tensor("wq", [D, 256], F32, kind="ExternalInput").ap()
    io0["wk"] = nc.dram_tensor("wk", [D, 256], F32, kind="ExternalInput").ap()
    io0["wv"] = nc.dram_tensor("wv", [D, 256], F32, kind="ExternalInput").ap()
    io0["bq"] = nc.dram_tensor("bq", [1, 256], F32, kind="ExternalInput").ap()
    io0["bk"] = nc.dram_tensor("bk", [1, 256], F32, kind="ExternalInput").ap()
    io0["bv"] = nc.dram_tensor("bv", [1, 256], F32, kind="ExternalInput").ap()
    io0["wo"] = nc.dram_tensor("wo", [D, D], F32, kind="ExternalInput").ap()
    io0["bo"] = nc.dram_tensor("bo", [1, D], F32, kind="ExternalInput").ap()
    io0["gw"] = nc.dram_tensor("gw", [D, 1024], I32, kind="ExternalInput").ap()
    io0["gs"] = nc.dram_tensor("gs", [D, 64], F32, kind="ExternalInput").ap()
    io0["uw"] = nc.dram_tensor("uw", [D, 1024], I32, kind="ExternalInput").ap()
    io0["us"] = nc.dram_tensor("us", [D, 64], F32, kind="ExternalInput").ap()
    io0["dw"] = nc.dram_tensor("dw", [256, S], I32, kind="ExternalInput").ap()
    io0["ds"] = nc.dram_tensor("ds", [256, 512], F32, kind="ExternalInput").ap()
    x1_t = nc.dram_tensor("x1_dram", [RPC, D], F32, kind="Internal")
    io0["x1_dram"] = x1_t.ap()
    io0["ident_t"] = ident_t
    io1 = {}
    io1["x1"] = x1_t.ap()
    io1["qkvw"] = nc.dram_tensor("qkvw", [D, 3072], F32, kind="ExternalInput").ap()
    io1["qkvb"] = nc.dram_tensor("qkvb", [1, 3072], F32, kind="ExternalInput").ap()
    io1["outw"] = nc.dram_tensor("outw", [D, D], F32, kind="ExternalInput").ap()
    io1["outb"] = nc.dram_tensor("outb", [1, D], F32, kind="ExternalInput").ap()
    io1["wfc"] = nc.dram_tensor("wfc", [H1, D], F32, kind="ExternalInput").ap()
    io1["wproj"] = nc.dram_tensor("wproj", [D, H1], F32, kind="ExternalInput").ap()
    io1["y"] = nc.dram_tensor("y", [RPC, D], F32, kind="ExternalOutput").ap()
    io1["g1_dram"] = nc.dram_tensor("g1_dram", [RPC, D], BF16, kind="Internal")
    io1["ident_t"] = ident_t
    with ExitStack() as g:
        emit_l0(p, g, io0, debug)
        p.retire_all_sems()
        emit_l1(p, g, io1, debug)
    return nc


def kernel(**inputs):
    X = np.asarray(inputs["L0_attn_X"], dtype=np.float32)
    x_sh = _shard_rows(X)
    nc = build_full_nc()
    in_maps = []
    for c in range(NCORES):
        in_maps.append(dict(
            x=x_sh[c],
            wq=np.ascontiguousarray(inputs["L0_attn_Wq"][:, c * 256:(c + 1) * 256]),
            wk=np.ascontiguousarray(inputs["L0_attn_Wk"][:, c * 256:(c + 1) * 256]),
            wv=np.ascontiguousarray(inputs["L0_attn_Wv"][:, c * 256:(c + 1) * 256]),
            bq=np.ascontiguousarray(inputs["L0_attn_bq"][c * 256:(c + 1) * 256].reshape(1, -1)),
            bk=np.ascontiguousarray(inputs["L0_attn_bk"][c * 256:(c + 1) * 256].reshape(1, -1)),
            bv=np.ascontiguousarray(inputs["L0_attn_bv"][c * 256:(c + 1) * 256].reshape(1, -1)),
            wo=np.asarray(inputs["L0_attn_Wo"]),
            bo=np.asarray(inputs["L0_attn_bo"]).reshape(1, -1),
            gw=np.ascontiguousarray(inputs["L0_ffn_gate_W_packed"][c * 2048:(c + 1) * 2048]),
            gs=np.ascontiguousarray(inputs["L0_ffn_gate_scales"][c * 2048:(c + 1) * 2048]),
            uw=np.ascontiguousarray(inputs["L0_ffn_up_W_packed"][c * 2048:(c + 1) * 2048]),
            us=np.ascontiguousarray(inputs["L0_ffn_up_scales"][c * 2048:(c + 1) * 2048]),
            dw=np.ascontiguousarray(inputs["L0_ffn_down_W_packed"][c * 256:(c + 1) * 256]),
            ds=np.ascontiguousarray(inputs["L0_ffn_down_scales"][c * 256:(c + 1) * 256]),
            qkvw=np.asarray(inputs["L1_attn_qkv_w"]),
            qkvb=np.asarray(inputs["L1_attn_qkv_b"]).reshape(1, -1),
            outw=np.asarray(inputs["L1_attn_out_w"]),
            outb=np.asarray(inputs["L1_attn_out_b"]).reshape(1, -1),
            wfc=np.asarray(inputs["L1_ffn_w_fc"]),
            wproj=np.asarray(inputs["L1_ffn_w_proj"]),
        ))
    res = run_bass_kernel_spmd(nc, in_maps, core_ids=list(range(NCORES)))
    out = np.empty((B, S, D), dtype=np.float32)
    for c in range(NCORES):
        y = res.results[c]["y"]
        out[0, c * PPC:(c + 1) * PPC] = y[:PPC]
        out[1, c * PPC:(c + 1) * PPC] = y[PPC:]
    return out


def _shard_rows(a):
    return [np.ascontiguousarray(np.concatenate(
        [a[0, c * PPC:(c + 1) * PPC], a[1, c * PPC:(c + 1) * PPC]], axis=0))
        for c in range(NCORES)]


def kernel_2launch(**inputs):
    X = np.asarray(inputs["L0_attn_X"], dtype=np.float32)
    x_sh = _shard_rows(X)

    nc0 = build_l0_nc()
    in_maps0 = []
    for c in range(NCORES):
        in_maps0.append(dict(
            x=x_sh[c],
            wq=np.ascontiguousarray(inputs["L0_attn_Wq"][:, c * 256:(c + 1) * 256]),
            wk=np.ascontiguousarray(inputs["L0_attn_Wk"][:, c * 256:(c + 1) * 256]),
            wv=np.ascontiguousarray(inputs["L0_attn_Wv"][:, c * 256:(c + 1) * 256]),
            bq=np.ascontiguousarray(inputs["L0_attn_bq"][c * 256:(c + 1) * 256].reshape(1, -1)),
            bk=np.ascontiguousarray(inputs["L0_attn_bk"][c * 256:(c + 1) * 256].reshape(1, -1)),
            bv=np.ascontiguousarray(inputs["L0_attn_bv"][c * 256:(c + 1) * 256].reshape(1, -1)),
            wo=np.asarray(inputs["L0_attn_Wo"]),
            bo=np.asarray(inputs["L0_attn_bo"]).reshape(1, -1),
            gw=np.ascontiguousarray(inputs["L0_ffn_gate_W_packed"][c * 2048:(c + 1) * 2048]),
            gs=np.ascontiguousarray(inputs["L0_ffn_gate_scales"][c * 2048:(c + 1) * 2048]),
            uw=np.ascontiguousarray(inputs["L0_ffn_up_W_packed"][c * 2048:(c + 1) * 2048]),
            us=np.ascontiguousarray(inputs["L0_ffn_up_scales"][c * 2048:(c + 1) * 2048]),
            dw=np.ascontiguousarray(inputs["L0_ffn_down_W_packed"][c * 256:(c + 1) * 256]),
            ds=np.ascontiguousarray(inputs["L0_ffn_down_scales"][c * 256:(c + 1) * 256]),
        ))
    res0 = run_bass_kernel_spmd(nc0, in_maps0, core_ids=list(range(NCORES)))
    x1_sh = [res0.results[c]["x1out"] for c in range(NCORES)]

    nc1 = build_l1_nc()
    in_maps1 = []
    for c in range(NCORES):
        in_maps1.append(dict(
            x1=np.ascontiguousarray(x1_sh[c]),
            qkvw=np.asarray(inputs["L1_attn_qkv_w"]),
            qkvb=np.asarray(inputs["L1_attn_qkv_b"]).reshape(1, -1),
            outw=np.asarray(inputs["L1_attn_out_w"]),
            outb=np.asarray(inputs["L1_attn_out_b"]).reshape(1, -1),
            wfc=np.asarray(inputs["L1_ffn_w_fc"]),
            wproj=np.asarray(inputs["L1_ffn_w_proj"]),
        ))
    res1 = run_bass_kernel_spmd(nc1, in_maps1, core_ids=list(range(NCORES)))

    out = np.empty((B, S, D), dtype=np.float32)
    for c in range(NCORES):
        y = res1.results[c]["y"]
        out[0, c * PPC:(c + 1) * PPC] = y[:PPC]
        out[1, c * PPC:(c + 1) * PPC] = y[PPC:]
    return out
